# version 04
# baseline (speedup 1.0000x reference)
import math
from contextlib import ExitStack

import numpy as np
import ml_dtypes
import concourse.bass as bass
import concourse.mybir as mybir
from concourse.bass_utils import run_bass_kernel_spmd

F32 = mybir.dt.float32
F16 = mybir.dt.float16
BF16 = mybir.dt.bfloat16
AF = mybir.ActivationFunctionType
ALU = mybir.AluOpType
AX = mybir.AxisListType

D = 2048
NH = 16
SEQ = 8192
QB = 512
HALF_JOBS = [(3, 0), (2, 1)]
HALF_PREFIX = [8192, 6144]
NE = 10
EH = 1408
NHS = 11
SQK = 1.0 / math.sqrt(128.0)


class Sem:
    def __init__(self, h):
        self.h = h
        self.n = 0


class Prog:

    def __init__(self):
        self.ops = []

    def op(self, eng, emit, waits=(), inc=None):
        idx = len(self.ops)
        ws = []
        for w in waits:
            if w is None:
                continue
            s, v, widx = w
            assert widx < idx, f"wait on future token at op {idx}"
            ws.append((s, v))
        incs = [] if inc is None else [inc]
        for s, n in incs:
            s.n += n
        self.ops.append((eng, ws, emit, incs))
        if incs:
            s, _ = incs[0]
            return (s, s.n, idx)
        return (None, 0, idx)

    def emit_stream(self, eng_obj, name):
        hiwater = {}
        for eng, ws, emit, incs in self.ops:
            if eng != name:
                continue
            for s, v in ws:
                if hiwater.get(id(s), -1) >= v:
                    continue
                eng_obj.wait_ge(s.h, v)
                hiwater[id(s)] = v
            if emit is None:
                continue
            ins = emit(eng_obj)
            for s, n in incs:
                ins.then_inc(s.h, n)


def W(*toks):
    return [t for t in toks if t is not None]


def build_program(taps=()):
    nc = bass.Bass("TRN2", target_bir_lowering=False)

    d_x = nc.dram_tensor("xseq", [SEQ, D], F32, kind="ExternalInput").ap()
    d_wq = nc.dram_tensor("wq", [D, D], F32, kind="ExternalInput").ap()
    d_bq = nc.dram_tensor("bq", [1, D], F32, kind="ExternalInput").ap()
    d_wk = nc.dram_tensor("wk", [D, 256], F32, kind="ExternalInput").ap()
    d_bk = nc.dram_tensor("bk", [1, 256], F32, kind="ExternalInput").ap()
    d_wv = nc.dram_tensor("wv_aug", [D, 260], F32, kind="ExternalInput").ap()
    d_bv = nc.dram_tensor("bv_aug", [1, 260], F32, kind="ExternalInput").ap()
    d_wo = nc.dram_tensor("wo", [D, D], F32, kind="ExternalInput").ap()
    d_bo = nc.dram_tensor("bo", [1, D], F32, kind="ExternalInput").ap()
    d_gw = nc.dram_tensor("gwr", [128, 16, 8], F32, kind="ExternalInput").ap()
    d_wg = nc.dram_tensor("wg", [NE, D, EH], F32, kind="ExternalInput").ap()
    d_wu = nc.dram_tensor("wu", [NE, D, EH], F32, kind="ExternalInput").ap()
    d_wd = nc.dram_tensor("wd", [NE, EH, D], F32, kind="ExternalInput").ap()
    d_mask = nc.dram_tensor("maskT", [2048, QB], BF16, kind="ExternalInput").ap()
    d_id = nc.dram_tensor("ident", [128, 128], F16, kind="ExternalInput").ap()
    d_out = nc.dram_tensor("out", [2048, D], F32, kind="ExternalOutput").ap()

    tap_outs = {}
    tap_dims = {
        "xn0": ([128, 16, 1024], F32), "K0": ([128, 2, 8192], F32),
        "V0": ([128, 64, 260], F32), "Q0": ([128, 16, 1024], F32),
        "mix0": ([128, 16, 1024], F32), "acc0w": ([128, 8, 2048], F32),
        "comb0": ([128, 64], F32), "acc0": ([128, 8, 2048], F32),
        "hp00": ([128, 11, 1024], F32),
    }
    for t in taps:
        shape, dt = tap_dims[t]
        tap_outs[t] = nc.dram_tensor("tap_" + t, shape, dt,
                                     kind="ExternalOutput").ap()

    p = Prog()

    with ExitStack() as st:
        E = st.enter_context
        A = nc.alloc_sbuf_tensor_at

        BASE = 16512
        g_off = [BASE]

        def galloc(name, shape, dtype):
            per = int(np.prod(shape[1:])) * mybir.dt.size(dtype)
            t = A(name, shape, dtype, offset=g_off[0])
            g_off[0] += (per + 31) // 32 * 32
            assert g_off[0] <= BASE + 19456, (name, g_off[0])
            return t

        ones = galloc("ones", [1, 512], F16)
        identS = galloc("identS", [128, 128], F16)
        bq_row = galloc("bq_row", [1, D], F16)
        bo_row = galloc("bo_row", [1, D], F16)
        bk_row = galloc("bk_row", [1, 256], F16)
        bv_row = galloc("bv_row", [1, 260], F16)
        gw_sb = galloc("gw_sb", [128, 16, 8], F16)
        qk_col = galloc("qk_col", [128, 1], F32)
        eps_c = galloc("eps_c", [128, 1], F32)
        stats = galloc("stats", [128, 24], F32)
        mv = [galloc("mv0", [128, 2], F32), galloc("mv1", [128, 2], F32)]
        rstd0 = [galloc("rstd00", [128, 1], F32), galloc("rstd01", [128, 1], F32)]
        rstd = [galloc("rstd_0", [128, 1], F32), galloc("rstd_1", [128, 1], F32)]
        rc = [galloc("rc0", [128, 1], F32), galloc("rc1", [128, 1], F32)]
        amix_tm = [galloc("amix0", [128, 128], F16), galloc("amix1", [128, 128], F16)]
        c_negm = galloc("c_negm", [128, 1], F32)
        c_e8 = galloc("c_e8", [128, 8], F32)
        c_s8 = galloc("c_s8", [128, 1], F32)
        c_rs8 = galloc("c_rs8", [128, 1], F32)
        c_pp8 = galloc("c_pp8", [128, 8], F32)
        c_m1 = galloc("c_m1", [128, 1], F32)
        c_ismax = galloc("c_ismax", [128, 8], F32)
        c_z8 = galloc("c_z8", [128, 8], F32)
        c_m2 = galloc("c_m2", [128, 1], F32)
        c_dum = galloc("c_dum", [128, 1], F32)
        combT = galloc("combT", [128, 64], F32)
        silu_t = [galloc("silu0", [128, 512], F16), galloc("silu1", [128, 512], F16)]
        onec = galloc("onec", [128, 1], F16)
        rs_row = galloc("rs_row", [1, 512], F16)
        rs_row32 = galloc("rs_row32", [1, 512], F32)
        bcast_sb = galloc("bcast_sb", [128, 512], F16)

        R1 = BASE + 19456
        R2 = R1 + 32768
        R3 = R2 + 33280
        R4 = R3 + 32768
        R5 = R4 + 32768
        R6 = R5 + 32768
        assert R6 + 22784 <= 229344

        K_sb = A("K_sb", [128, 2, 8192], F16, offset=R1)
        V_sb = A("V_sb", [128, 64, 260], F16, offset=R2)
        acc = A("acc", [128, 8, D], F32, offset=R1)
        xn = A("xn", [128, 16, 1024], F16, offset=R3)
        x_t = [A("x_t0", [128, D], F32, offset=R4),
               A("x_t1", [128, D], F32, offset=R4 + 8192)]
        xn_tm = [A("xn_tm0", [128, D], F16, offset=R4 + 16384),
                 A("xn_tm1", [128, D], F16, offset=R4 + 20480)]
        xn_sc = [A("xn_sc0", [128, 16, 128], F16, offset=R4 + 24576),
                 A("xn_sc1", [128, 16, 128], F16, offset=R4 + 28672)]
        Q_sb = A("Q_sb", [128, 16, 1024], F16, offset=R4)
        wo_t = [A("wo_t0", [128, 16, 512], F16, offset=R4),
                A("wo_t1", [128, 16, 512], F16, offset=R4 + 16384)]
        wg_t = [A("wg_t0", [128, 16, 256], F16, offset=R4),
                A("wg_t1", [128, 16, 256], F16, offset=R4 + 8192)]
        wu_t = [A("wu_t0", [128, 16, 256], F16, offset=R4 + 16384),
                A("wu_t1", [128, 16, 256], F16, offset=R4 + 24576)]
        amix = A("amix", [128, 16, 1024], F16, offset=R5)
        hp = A("hp", [128, 11, 1024], F16, offset=R5)
        wk_sb = A("wk_sb", [128, 16, 256], F16, offset=R6)
        wv_sb = A("wv_sb", [128, 16, 260], F16, offset=R6 + 8192)
        wq_t = [A("wq_t0", [128, 16, 256], F16, offset=R6),
                A("wq_t1", [128, 16, 256], F16, offset=R6 + 8192)]
        mask_sb = A("mask_sb", [128, 16, 512], BF16, offset=R6)
        expT = [A("expT%d" % i, [128, 1024], BF16, offset=R6 + 16384 + 2048 * i)
                for i in range(2)]
        xw_t = [A("xw_t0", [128, D], F32, offset=R6),
                A("xw_t1", [128, D], F32, offset=R6 + 8192)]
        wd_t = [A("wd_t0", [128, 11, 512], F16, offset=R6),
                A("wd_t1", [128, 11, 512], F16, offset=R6 + 11264)]

        parena = E(nc.psum_tensor("parena", [128, 3072], F32))
        p32 = [parena[:, 512 * i:512 * i + 512] for i in range(6)]
        pair = [parena[:, 1024 * j:1024 * j + 1024] for j in range(2)]
        av_bank = parena[:, 2048:2560]
        sums_v = parena[0:1, 2560:3072]
        p16 = [E(nc.psum_tensor("pt%d" % i, [128, 1024], F16)) for i in range(2)]

        s_x = Sem(E(nc.semaphore("s_x")))
        s_w = Sem(E(nc.semaphore("s_w")))
        s_pe = Sem(E(nc.semaphore("s_pe")))
        s_act = Sem(E(nc.semaphore("s_act")))
        s_dve = Sem(E(nc.semaphore("s_dve")))
        s_out = Sem(E(nc.semaphore("s_out")))

        p.op("dve", lambda e: e.memset(ones[:], 1.0))
        p.op("dve", lambda e: e.memset(onec[:], 1.0))
        p.op("dve", lambda e: e.memset(qk_col[:], SQK))
        tok_init = p.op("dve", lambda e: e.memset(eps_c[:], 1e-5), inc=(s_dve, 1))

        def gdma(dst, src, waits=()):
            return p.op("pool", lambda e, d=dst, s=src: e.dma_start(d, s),
                        waits=waits, inc=(s_w, 16))

        gdma(bq_row[:], d_bq[:, :])
        gdma(bo_row[:], d_bo[:, :])
        gdma(bk_row[:], d_bk[:, :])
        gdma(bv_row[:], d_bv[:, :])
        gdma(gw_sb[:], d_gw[:, :, :])
        tok_globals = gdma(identS[:], d_id[:, :])

        free = {}
        tap_done = set()

        def tap(name, sbuf_ap, wait_tok):
            if name in tap_outs and name not in tap_done:
                tap_done.add(name)
                p.op("pool", lambda e, n=name, a=sbuf_ap: e.dma_start(
                    tap_outs[n][...], a), waits=[wait_tok], inc=(s_out, 16))

        tpsum_free = {0: [], 1: []}

        for half in range(2):
            mlist = HALF_JOBS[half]
            prefix = HALF_PREFIX[half]
            ntiles = prefix // 128
            own_blocks = [4 * mlist[0], 4 * mlist[1]]
            own_tiles = {}
            for j, b in enumerate(own_blocks):
                for tb in range(4):
                    own_tiles[4 * b + tb] = 512 * j + 128 * tb

            if half == 1:
                bar = (s_out, 16 * 1 + 16 * len(tap_done), len(p.ops) - 1)
                for eng in ("sp", "pool", "pe", "act", "dve"):
                    p.op(eng, None, waits=[bar])

            tok_wk = gdma(wk_sb[:], d_wk.rearrange("(c p) d -> p c d", p=128))
            tok_wv = gdma(wv_sb[:], d_wv.rearrange("(c p) d -> p c d", p=128))

            kpsum_free = {0: [], 1: []}
            vpsum_free = {0: [], 1: []}
            kv_evacs = []
            last_xn_evac = None
            for i in range(ntiles):
                sl = i % 2
                tok_x = p.op("sp", lambda e, sl=sl, i=i: e.dma_start(
                    x_t[sl][:], d_x[128 * i:128 * i + 128, :]),
                    waits=W(free.get("x_t%d" % sl)), inc=(s_x, 16))
                tok_st = None
                for k in range(4):
                    tok_st = p.op("dve", lambda e, sl=sl, k=k: e.bn_stats(
                        stats[:, 6 * k:6 * k + 6],
                        x_t[sl][:, 512 * k:512 * k + 512]),
                        waits=[tok_x] if k == 0 else (),
                        inc=(s_dve, 1) if k == 3 else None)
                tok_ag = p.op("dve", lambda e, sl=sl: e.bn_aggr(mv[sl][:], stats[:]),
                              waits=W(tok_st, free.get("mv%d" % sl)),
                              inc=(s_dve, 1))
                tok_sq = p.op("act", lambda e, sl=sl: e.activation(
                    rstd0[sl][:], mv[sl][:, 1:2], AF.Sqrt, bias=eps_c[:]),
                    waits=W(tok_ag, tok_init, free.get("rstd0%d" % sl)),
                    inc=(s_act, 1))
                free["mv%d" % sl] = tok_sq
                tok_rc = p.op("dve", lambda e, sl=sl: e.reciprocal(
                    rstd[sl][:], rstd0[sl][:]), waits=[tok_sq], inc=(s_dve, 1))
                free["rstd0%d" % sl] = tok_rc
                tok_ts = p.op("dve", lambda e, sl=sl: e.tensor_scalar(
                    xn_tm[sl][:], x_t[sl][:], mv[sl][:, 0:1], rstd[sl][:],
                    ALU.subtract, ALU.mult),
                    waits=W(tok_rc, free.get("xn_tm%d" % sl)), inc=(s_dve, 1))
                free["x_t%d" % sl] = tok_ts

                if i in own_tiles:
                    off = own_tiles[i]
                    dst_ap = lambda c0, off=off: xn[:, c0:c0 + 8, off:off + 128]
                    src = lambda fc, off=off: xn[:, fc, off:off + 128]
                else:
                    dst_ap = lambda c0, sl=sl: xn_sc[sl][:, c0:c0 + 8, :]
                    src = lambda fc, sl=sl: xn_sc[sl][:, fc, :]

                tr_toks = []
                for b16 in range(2):
                    wt = W(tok_ts, tok_globals) + tpsum_free[b16]
                    tk = None
                    for k in range(8):
                        fc = 8 * b16 + k
                        tk = p.op("pe", lambda e, sl=sl, b16=b16, k=k, fc=fc:
                                  e.transpose(p16[b16][:, 128 * k:128 * k + 128],
                                              xn_tm[sl][:, 128 * fc:128 * fc + 128],
                                              identS[:]),
                                  waits=wt if k == 0 else (),
                                  inc=(s_pe, 1) if k == 7 else None)
                    tr_toks.append(tk)
                free["xn_tm%d" % sl] = tr_toks[1]
                ev_last = None
                for b16 in range(2):
                    wt = [tr_toks[b16]]
                    if i in own_tiles:
                        wt = wt + W(free.get("xn_own"))
                    ev_last = p.op("dve", lambda e, b16=b16, dst_ap=dst_ap:
                                   e.tensor_copy(
                                       dst_ap(8 * b16),
                                       p16[b16].rearrange("p (c t) -> p c t", c=8)),
                                   waits=wt, inc=(s_dve, 1))
                    tpsum_free[b16] = [ev_last]
                xn_ready = ev_last
                last_xn_evac = ev_last

                par = i % 2
                bankK = [p32[0 + par], p32[2 + par]]
                bankV = p32[4 + par]
                for h in range(2):
                    wt = W(xn_ready, tok_wk) + kpsum_free[par]
                    tk = None
                    p.op("pe", lambda e, h=h, par=par: e.matmul(
                        bankK[h][:, 0:128] if False else p32[[0, 2][h] + par][:, 0:128],
                        bk_row[0:1, 128 * h:128 * h + 128], ones[0:1, 0:128],
                        start=True, stop=False), waits=wt)
                    for fc in range(16):
                        tk = p.op("pe", lambda e, h=h, par=par, fc=fc, src=src:
                                  e.matmul(p32[[0, 2][h] + par][:, 0:128],
                                           wk_sb[:, fc, 128 * h:128 * h + 128],
                                           src(fc), start=False, stop=(fc == 15)),
                                  inc=(s_pe, 1) if fc == 15 else None)
                    tok_kev = p.op("act", lambda e, h=h, par=par, i=i: e.activation(
                        K_sb[:, h, 128 * i:128 * i + 128],
                        p32[[0, 2][h] + par][:, 0:128], AF.Copy),
                        waits=[tk], inc=(s_act, 1))
                    if h == 0:
                        kpsum_free[par] = [tok_kev]
                    else:
                        kpsum_free[par] = kpsum_free[par] + [tok_kev]
                    kv_evacs.append(tok_kev)
                wt = W(xn_ready, tok_wv) + vpsum_free[par]
                p.op("pe", lambda e, par=par: e.matmul(
                    p32[4 + par][:, 0:260], ones[0:1, 0:128], bv_row[0:1, :],
                    start=True, stop=False), waits=wt)
                tk = None
                for fc in range(16):
                    tk = p.op("pe", lambda e, par=par, fc=fc, src=src: e.matmul(
                        p32[4 + par][:, 0:260], src(fc), wv_sb[:, fc, :],
                        start=False, stop=(fc == 15)),
                        inc=(s_pe, 1) if fc == 15 else None)
                tok_vev = p.op("dve", lambda e, par=par, i=i: e.tensor_copy(
                    V_sb[:, i, :], p32[4 + par][:, 0:260]),
                    waits=[tk], inc=(s_dve, 1))
                vpsum_free[par] = [tok_vev]
                kv_evacs.append(tok_vev)
            free["xn_own"] = None
            kv_last_dve = tok_vev
            kv_last_act = kv_evacs[-2]
            if half == 0:
                tap("xn0", xn[:, :, :], last_xn_evac)
                tap("K0", K_sb[:, :, :], kv_last_act)
                tap("V0", V_sb[:, :, :], kv_last_dve)

            qbank_free = {k: kpsum_free[0] + kpsum_free[1] for k in range(4)}
            qbank_free[4] = vpsum_free[0]
            qbank_free[5] = vpsum_free[1]
            q_done = None
            bk_ctr = 0
            for qc in range(8):
                sl = qc % 2
                wqw = W(free.get("wq%d" % sl))
                if qc < 2:
                    wqw = wqw + [kv_last_dve]
                tok_wq = gdma(wq_t[sl][:],
                              d_wq[:, 256 * qc:256 * qc + 256]
                              .rearrange("(c p) q -> p c q", p=128),
                              waits=wqw)
                for hl in range(2):
                    hd = 2 * qc + hl
                    for tg in range(2):
                        bk = bk_ctr % 4
                        bk_ctr += 1
                        wt = W(tok_wq, kv_last_dve, kv_last_act) + qbank_free[bk]
                        p.op("pe", lambda e, hd=hd, bk=bk: e.matmul(
                            p32[bk][:, 0:512],
                            bq_row[0:1, 128 * hd:128 * hd + 128],
                            ones[0:1, 0:512], start=True, stop=False), waits=wt)
                        tk = None
                        for fc in range(16):
                            tk = p.op("pe", lambda e, bk=bk, fc=fc, sl=sl, hl=hl,
                                      tg=tg: e.matmul(
                                p32[bk][:, 0:512],
                                wq_t[sl][:, fc, 128 * hl:128 * hl + 128],
                                xn[:, fc, 512 * tg:512 * tg + 512],
                                start=False, stop=(fc == 15)),
                                inc=(s_pe, 1) if fc == 15 else None)
                        free["wq%d" % sl] = tk
                        tok_qe = p.op("act", lambda e, hd=hd, bk=bk, tg=tg:
                                      e.activation(
                                          Q_sb[:, hd, 512 * tg:512 * tg + 512],
                                          p32[bk][:, 0:512], AF.Copy,
                                          scale=qk_col[:]),
                                      waits=[tk, tok_init], inc=(s_act, 1))
                        qbank_free[bk] = [tok_qe]
            q_last = tok_qe
            if half == 0:
                tap("Q0", Q_sb[:, :, :], q_last)

            tok_mask = gdma(mask_sb[:], d_mask.rearrange("(c p) q -> p c q", p=128),
                            waits=W(free.get("wq0"), free.get("wq1")))
            pair_free = {0: qbank_free[0] + qbank_free[1],
                         1: qbank_free[2] + qbank_free[3]}
            av_free = vpsum_free[0]
            sums_free = vpsum_free[1]
            expT_free = {k: [] for k in range(2)}
            tok_me = None
            for slot in range(2):
                m = mlist[slot]
                kcs = (2048 * (m + 1)) // 128
                qb0 = 512 * slot
                for hd in range(NH):
                    kvh = hd // 8
                    tok_av_last = None
                    tok_sum_last = None
                    for kp in range(kcs // 2):
                        pb = kp % 2
                        eb = kp % 2
                        wt = W(q_last, kv_last_dve) + pair_free[pb]
                        for half_k in range(2):
                            kc = 2 * kp + half_k
                            tok_qk = p.op("pe", lambda e, kvh=kvh, hd=hd, kc=kc,
                                          pb=pb, hk=half_k, qb0=qb0: e.matmul(
                                pair[pb][:, 512 * hk:512 * hk + 512],
                                K_sb[:, kvh, 128 * kc:128 * kc + 128],
                                Q_sb[:, hd, qb0:qb0 + 512], start=True, stop=True),
                                waits=wt if half_k == 0 else (),
                                inc=(s_pe, 1) if half_k == 1 else None)
                        tok_ex = p.op("act", lambda e, pb=pb, eb=eb: e.activation(
                            expT[eb][:], pair[pb][:, 0:1024], AF.Exp),
                            waits=[tok_qk] + expT_free[eb], inc=(s_act, 1))
                        pair_free[pb] = [tok_ex]
                        tok_rdy = tok_ex
                        if 2 * kp >= kcs - 16:
                            mi = 2 * kp - (kcs - 16)
                            tok_rdy = p.op("dve", lambda e, eb=eb, mi=mi:
                                           e.tensor_tensor(
                                               expT[eb][:], expT[eb][:],
                                               mask_sb[:, mi:mi + 2, :]
                                               .rearrange("p c q -> p (c q)"),
                                               ALU.mult),
                                           waits=[tok_ex, tok_mask],
                                           inc=(s_dve, 1))
                        tk = None
                        for half_k in range(2):
                            kc = 2 * kp + half_k
                            wt2 = [tok_rdy]
                            if kc == 0:
                                wt2 = wt2 + av_free
                            tk = p.op("pe", lambda e, eb=eb, kc=kc, kvh=kvh,
                                      hk=half_k, kcs=kcs: e.matmul(
                                av_bank[:, 0:512],
                                V_sb[:, kc, 130 * kvh:130 * kvh + 128],
                                expT[eb][:, 512 * hk:512 * hk + 512],
                                start=(kc == 0), stop=(kc == kcs - 1)),
                                waits=wt2, inc=(s_pe, 1) if kc == kcs - 1 else None)
                            wt3 = []
                            if kc == 0:
                                wt3 = sums_free
                            tok_sum_last = p.op("pe", lambda e, eb=eb, kc=kc,
                                                hk=half_k, kcs=kcs: e.matmul(
                                sums_v, onec[:, 0:1],
                                expT[eb][:, 512 * hk:512 * hk + 512],
                                start=(kc == 0), stop=(kc == kcs - 1)),
                                waits=wt3,
                                inc=(s_pe, 1) if half_k == 1 else None)
                        expT_free[eb] = [tok_sum_last]
                        tok_av_last = tk
                    tok_rr = p.op("dve", lambda e: e.reciprocal(
                        rs_row32[:], sums_v), waits=W(tok_sum_last,
                                                      free.get("rs_row32")),
                        inc=(s_dve, 1))
                    sums_free = [tok_rr]
                    tok_rr = p.op("act", lambda e: e.activation(
                        rs_row[:], rs_row32[:], AF.Copy),
                        waits=W(tok_rr, free.get("rs_row")), inc=(s_act, 1))
                    free["rs_row32"] = tok_rr
                    bcb = (kcs // 2) % 2
                    tok_bc = p.op("pe", lambda e, bcb=bcb: e.matmul(
                        pair[bcb][:, 0:512], ones[0:1, 0:128], rs_row[:],
                        start=True, stop=True),
                        waits=[tok_rr] + pair_free[bcb], inc=(s_pe, 1))
                    tok_bcc = p.op("act", lambda e, bcb=bcb: e.activation(
                        bcast_sb[:], pair[bcb][:, 0:512], AF.Copy),
                        waits=W(tok_bc, free.get("bcast_sb")), inc=(s_act, 1))
                    free["rs_row"] = tok_bcc
                    pair_free[bcb] = [tok_bcc]
                    tok_me = p.op("dve", lambda e, hd=hd, qb0=qb0:
                                  e.tensor_tensor(
                                      amix[:, hd, qb0:qb0 + 512],
                                      av_bank[:, 0:512], bcast_sb[:],
                                      ALU.mult),
                                  waits=W(tok_av_last, tok_bcc), inc=(s_dve, 1))
                    free["bcast_sb"] = tok_me
                    av_free = [tok_me]
            mix_last = tok_me
            if half == 0:
                tap("mix0", amix[:, :, :], mix_last)

            wo_bank_free = {k: [] for k in range(6)}
            wo_pe_last = None
            tok_ae = None
            for dmc in range(4):
                sl = dmc % 2
                tok_wo = gdma(wo_t[sl][:],
                              d_wo[:, 512 * dmc:512 * dmc + 512]
                              .rearrange("(c p) q -> p c q", p=128),
                              waits=W(free.get("wo%d" % sl), mix_last))
                for t8 in range(8):
                    bk = (dmc * 8 + t8) % 6
                    wt = W(tok_wo, mix_last) + wo_bank_free[bk]
                    p.op("pe", lambda e, dmc=dmc, bk=bk: e.matmul(
                        p32[bk][:, 0:512], ones[0:1, 0:128],
                        bo_row[0:1, 512 * dmc:512 * dmc + 512],
                        start=True, stop=False), waits=wt)
                    tk = None
                    for fc in range(16):
                        tk = p.op("pe", lambda e, bk=bk, fc=fc, sl=sl, t8=t8:
                                  e.matmul(p32[bk][:, 0:512],
                                           amix[:, fc, 128 * t8:128 * t8 + 128],
                                           wo_t[sl][:, fc, :],
                                           start=False, stop=(fc == 15)),
                                  inc=(s_pe, 1) if fc == 15 else None)
                    free["wo%d" % sl] = tk
                    wo_pe_last = tk
                    tok_ae = p.op("dve", lambda e, bk=bk, t8=t8, dmc=dmc:
                                  e.tensor_copy(
                                      acc[:, t8, 512 * dmc:512 * dmc + 512],
                                      p32[bk][:, 0:512]),
                                  waits=[tk], inc=(s_dve, 1))
                    wo_bank_free[bk] = [tok_ae]
            acc_w_last = tok_ae
            tok_ax = None
            for t8 in range(8):
                sl = t8 % 2
                j, tb = t8 // 4, t8 % 4
                row0 = 512 * own_blocks[j] + 128 * tb
                tok_xw = p.op("sp", lambda e, sl=sl, row0=row0: e.dma_start(
                    xw_t[sl][:], d_x[row0:row0 + 128, :]),
                    waits=W(free.get("xw%d" % sl), mix_last), inc=(s_x, 16))
                tok_ax = p.op("dve", lambda e, sl=sl, t8=t8: e.tensor_tensor(
                    acc[:, t8, :], acc[:, t8, :], xw_t[sl][:], ALU.add),
                    waits=[tok_xw, acc_w_last], inc=(s_dve, 1))
                free["xw%d" % sl] = tok_ax
            accx_last = tok_ax
            if half == 0:
                tap("acc0w", acc[:, :, :], accx_last)

            comb_prev = None
            tok_cb = None
            for t8 in range(8):
                bk = t8 % 2
                wt = W(tok_globals, comb_prev) + wo_bank_free[bk]
                tk = None
                for fc in range(16):
                    tk = p.op("pe", lambda e, bk=bk, fc=fc, t8=t8: e.matmul(
                        p32[bk][:, 0:8], xn[:, fc, 128 * t8:128 * t8 + 128],
                        gw_sb[:, fc, :], start=(fc == 0), stop=(fc == 15)),
                        waits=wt if fc == 0 else (),
                        inc=(s_pe, 1) if fc == 15 else None)
                tok_nm = p.op("dve", lambda e, bk=bk: e.reduce_max(
                    c_negm[:], p32[bk][:, 0:8], AX.X, negate=True),
                    waits=W(tk, comb_prev), inc=(s_dve, 1))
                tok_e8 = p.op("act", lambda e, bk=bk: e.activation(
                    c_e8[:], p32[bk][:, 0:8], AF.Exp, bias=c_negm[:],
                    accum_out=c_s8[:]), waits=[tok_nm], inc=(s_act, 1))
                tok_e8b = p.op("act", lambda e: e.activation(
                    c_dum[:], c_negm[:], AF.Square), waits=[tok_e8],
                    inc=(s_act, 1))
                wo_bank_free[bk] = [tok_e8b]
                tok_rs = p.op("dve", lambda e: e.reciprocal(c_rs8[:], c_s8[:]),
                              waits=[tok_e8b], inc=(s_dve, 1))
                tok_pp = p.op("dve", lambda e: e.tensor_scalar(
                    c_pp8[:], c_e8[:], c_rs8[:], None, ALU.mult),
                    waits=[tok_rs], inc=(s_dve, 1))
                tok_m1 = p.op("dve", lambda e: e.reduce_max(
                    c_m1[:], c_pp8[:], AX.X), waits=[tok_pp], inc=(s_dve, 1))
                tok_im = p.op("dve", lambda e: e.tensor_scalar(
                    c_ismax[:], c_pp8[:], c_m1[:], None, ALU.is_ge),
                    waits=[tok_m1], inc=(s_dve, 1))
                tok_z8 = p.op("dve", lambda e: e.scalar_tensor_tensor(
                    c_z8[:], c_ismax[:], -1e30, c_pp8[:], ALU.mult, ALU.add),
                    waits=[tok_im], inc=(s_dve, 1))
                tok_m2 = p.op("dve", lambda e: e.reduce_max(
                    c_m2[:], c_z8[:], AX.X), waits=[tok_z8], inc=(s_dve, 1))
                tok_cb = p.op("dve", lambda e, t8=t8: e.scalar_tensor_tensor(
                    combT[:, 8 * t8:8 * t8 + 8], c_pp8[:], c_m2[:], c_pp8[:],
                    ALU.is_ge, ALU.mult), waits=[tok_m2], inc=(s_dve, 1))
                comb_prev = tok_cb
            comb_last = tok_cb
            if half == 0:
                tap("comb0", combT[:, :], comb_last)

            gu_bank_free = {k: [comb_last] for k in range(4)}
            dn_bank_free = {4: wo_bank_free.get(4, []) + [comb_last],
                            5: wo_bank_free.get(5, []) + [comb_last]}
            hp_free = [wo_pe_last]
            acc_chain = [accx_last, comb_last]
            tok_de = None
            for e_i in range(NE):
                hcl = [(256 * c, 256 if c < 5 else 128) for c in range(6)]
                tok_hp = None
                first_gu = (e_i == 0)
                for ci, (hoff, cw) in enumerate(hcl):
                    sl = ci % 2
                    wgw = W(free.get("wg%d" % sl))
                    wuw = W(free.get("wu%d" % sl))
                    if first_gu and ci < 2:
                        wgw += W(free.get("wo0"), free.get("wo1"))
                        wuw += W(free.get("wo0"), free.get("wo1"))
                    tok_wg = gdma(wg_t[sl][:, :, 0:cw],
                                  d_wg[e_i, :, hoff:hoff + cw]
                                  .rearrange("(c p) h -> p c h", p=128), waits=wgw)
                    tok_wu = gdma(wu_t[sl][:, :, 0:cw],
                                  d_wu[e_i, :, hoff:hoff + cw]
                                  .rearrange("(c p) h -> p c h", p=128), waits=wuw)
                    for hs in range(cw // 128):
                        hglob = hoff // 128 + hs
                        for tg in range(2):
                            gb = 2 * ((hglob + tg) % 2)
                            spr = (hglob + tg) % 2
                            wt = W(tok_wg, comb_last) + gu_bank_free[gb]
                            tkg = None
                            for fc in range(16):
                                tkg = p.op("pe", lambda e, gb=gb, fc=fc, sl=sl,
                                           hs=hs, tg=tg: e.matmul(
                                    p32[gb][:, 0:512],
                                    wg_t[sl][:, fc, 128 * hs:128 * hs + 128],
                                    xn[:, fc, 512 * tg:512 * tg + 512],
                                    start=(fc == 0), stop=(fc == 15)),
                                    waits=wt if fc == 0 else (),
                                    inc=(s_pe, 1) if fc == 15 else None)
                            wt = W(tok_wu) + gu_bank_free[gb + 1]
                            tku = None
                            for fc in range(16):
                                tku = p.op("pe", lambda e, gb=gb, fc=fc, sl=sl,
                                           hs=hs, tg=tg: e.matmul(
                                    p32[gb + 1][:, 0:512],
                                    wu_t[sl][:, fc, 128 * hs:128 * hs + 128],
                                    xn[:, fc, 512 * tg:512 * tg + 512],
                                    start=(fc == 0), stop=(fc == 15)),
                                    waits=wt if fc == 0 else (),
                                    inc=(s_pe, 1) if fc == 15 else None)
                            free["wg%d" % sl] = tkg
                            free["wu%d" % sl] = tku
                            tok_si = p.op("act", lambda e, gb=gb, spr=spr:
                                          e.activation(silu_t[spr][:],
                                                       p32[gb][:, 0:512], AF.Silu),
                                          waits=W(tkg, free.get("silu%d" % spr)),
                                          inc=(s_act, 1))
                            gu_bank_free[gb] = [tok_si]
                            tok_hp = p.op("dve", lambda e, hglob=hglob, tg=tg,
                                          gb=gb, spr=spr: e.tensor_tensor(
                                hp[:, hglob, 512 * tg:512 * tg + 512],
                                silu_t[spr][:], p32[gb + 1][:, 0:512], ALU.mult),
                                waits=W(tku, tok_si) + hp_free, inc=(s_dve, 1))
                            free["silu%d" % spr] = tok_hp
                            gu_bank_free[gb + 1] = [tok_hp]
                hp_ready = tok_hp
                if half == 0 and e_i == 0:
                    tap("hp00", hp[:, :, :], hp_ready)
                hp_pe_last = None
                for dmc in range(4):
                    sl = dmc % 2
                    wdw = W(free.get("wd%d" % sl))
                    if e_i == 0 and dmc < 2:
                        wdw += [mix_last, accx_last]
                    tok_wd = gdma(wd_t[sl][:],
                                  d_wd[e_i, :, 512 * dmc:512 * dmc + 512]
                                  .rearrange("(c p) d -> p c d", p=128), waits=wdw)
                    for t8 in range(8):
                        bk = 4 + (t8 % 2)
                        wt = W(tok_wd, hp_ready) + dn_bank_free[bk]
                        tk = None
                        for hs in range(NHS):
                            tk = p.op("pe", lambda e, bk=bk, hs=hs, t8=t8, sl=sl:
                                      e.matmul(p32[bk][:, 0:512],
                                               hp[:, hs, 128 * t8:128 * t8 + 128],
                                               wd_t[sl][:, hs, :],
                                               start=(hs == 0),
                                               stop=(hs == NHS - 1)),
                                      waits=wt if hs == 0 else (),
                                      inc=(s_pe, 1) if hs == NHS - 1 else None)
                        hp_pe_last = tk
                        free["wd%d" % sl] = tk
                        wt = [tk] + acc_chain
                        acc_chain = []
                        if e_i < 8:
                            tok_de = p.op("dve", lambda e, bk=bk, t8=t8, dmc=dmc,
                                          e_i=e_i: e.scalar_tensor_tensor(
                                acc[:, t8, 512 * dmc:512 * dmc + 512],
                                p32[bk][:, 0:512],
                                combT[:, 8 * t8 + e_i:8 * t8 + e_i + 1],
                                acc[:, t8, 512 * dmc:512 * dmc + 512],
                                ALU.mult, ALU.add), waits=wt, inc=(s_dve, 1))
                        else:
                            tok_de = p.op("dve", lambda e, bk=bk, t8=t8, dmc=dmc:
                                          e.tensor_tensor(
                                acc[:, t8, 512 * dmc:512 * dmc + 512],
                                acc[:, t8, 512 * dmc:512 * dmc + 512],
                                p32[bk][:, 0:512], ALU.add), waits=wt,
                                inc=(s_dve, 1))
                        dn_bank_free[bk] = [tok_de]
                hp_free = [hp_pe_last]
                acc_chain = [tok_de]
            ffn_last = tok_de
            if half == 0:
                tap("acc0", acc[:, :, :], ffn_last)

            out_view = d_out[1024 * half:1024 * half + 1024, :] \
                .rearrange("(t p) d -> p t d", p=128)
            tok_out = p.op("sp", lambda e, ov=out_view: e.dma_start(ov, acc[:]),
                           waits=[ffn_last], inc=(s_out, 16))
            for nm in ("x_t0", "x_t1", "xw0", "xw1", "wo0", "wo1", "wg0", "wg1",
                       "wu0", "wu1", "wd0", "wd1", "wq0", "wq1", "xn_tm0",
                       "xn_tm1", "mv0", "mv1", "rstd00", "rstd01", "rc0", "rc1",
                       "amix0", "amix1", "silu0", "silu1", "xn_own"):
                free[nm] = None
            tpsum_free = {0: [], 1: []}

        p.op("sp", None, waits=[(s_out, s_out.n, len(p.ops) - 1)])

        with nc.Block() as blk:
            @blk.sync
            def _(e):
                p.emit_stream(e, "sp")

            @blk.gpsimd
            def _(e):
                p.emit_stream(e, "pool")

            @blk.tensor
            def _(e):
                p.emit_stream(e, "pe")

            @blk.scalar
            def _(e):
                p.emit_stream(e, "act")

            @blk.vector
            def _(e):
                p.emit_stream(e, "dve")

    return nc


def make_in_maps(inputs):
    f32 = lambda a: np.ascontiguousarray(np.asarray(a, dtype=np.float32))
    x = f32(inputs["L0_attn_x"])
    wq = f32(inputs["L0_attn_Wq"])
    bq = f32(inputs["L0_attn_bq"]).reshape(1, D)
    wk = f32(inputs["L0_attn_Wk"])
    bk = f32(inputs["L0_attn_bk"]).reshape(1, 256)
    wv = f32(inputs["L0_attn_Wv"])
    bv = f32(inputs["L0_attn_bv"]).reshape(256)
    wo = f32(inputs["L0_attn_Wo"])
    bo = f32(inputs["L0_attn_bo"]).reshape(1, D)
    gate_w = f32(inputs["L0_ffn_gate_w"])
    w_gate = f32(inputs["L0_ffn_w_gate"])
    w_up = f32(inputs["L0_ffn_w_up"])
    w_down = f32(inputs["L0_ffn_w_down"])
    sw_gate = f32(inputs["L0_ffn_sw_gate"])
    sw_up = f32(inputs["L0_ffn_sw_up"])
    sw_down = f32(inputs["L0_ffn_sw_down"])

    wv_aug = np.zeros((D, 260), np.float32)
    wv_aug[:, 0:128] = wv[:, 0:128]
    wv_aug[:, 130:258] = wv[:, 128:256]
    bv_aug = np.zeros((1, 260), np.float32)
    bv_aug[0, 0:128] = bv[0:128]
    bv_aug[0, 128] = 1.0
    bv_aug[0, 130:258] = bv[128:256]
    bv_aug[0, 258] = 1.0
    gwr = np.ascontiguousarray(gate_w.reshape(16, 128, 8).transpose(1, 0, 2))
    wg = np.ascontiguousarray(np.concatenate(
        [w_gate, sw_gate[:, 0:EH][None], sw_gate[:, EH:2 * EH][None]], 0))
    wu = np.ascontiguousarray(np.concatenate(
        [w_up, sw_up[:, 0:EH][None], sw_up[:, EH:2 * EH][None]], 0))
    wd = np.ascontiguousarray(np.concatenate(
        [w_down, sw_down[0:EH, :][None], sw_down[EH:2 * EH, :][None]], 0))
    ident = np.eye(128, dtype=np.float16)

    shared = dict(wq=wq, bq=bq, wk=wk, bk=bk, wv_aug=wv_aug, bv_aug=bv_aug,
                  wo=wo, bo=bo, gwr=gwr, wg=wg, wu=wu, wd=wd, ident=ident)
    in_maps = []
    kq = np.arange(512)
    for c in range(8):
        s, r = c // 4, c % 4
        perm = np.arange(16)
        if r:
            for m in range(4):
                perm[4 * m], perm[4 * m + r] = perm[4 * m + r], perm[4 * m]
        xp = np.ascontiguousarray(
            x[s].reshape(16, 512, D)[perm].reshape(SEQ, D))
        mask = np.zeros((2048, 512), np.float32)
        mask[0:512] = (kq[:, None] <= kq[None, :])
        for j in range(1, 4):
            mask[512 * j:512 * j + 512] = 1.0 if j <= r else 0.0
        m16 = mask.astype(ml_dtypes.bfloat16)
        im = dict(shared)
        im["xseq"] = xp
        im["maskT"] = m16
        in_maps.append(im)
    return in_maps


def assemble_out(results):
    out = np.empty((2, SEQ, D), np.float32)
    for c in range(8):
        o = np.asarray(results[c]["out"], np.float32)
        s, r = c // 4, c % 4
        for row0, m in ((0, 3), (512, 0), (1024, 2), (1536, 1)):
            b = 4 * m + r
            out[s, 512 * b:512 * b + 512] = o[row0:row0 + 512]
    return out


def kernel(**inputs):
    in_maps = make_in_maps(inputs)
    nc = build_program()
    res = run_bass_kernel_spmd(nc, in_maps, core_ids=list(range(8)))
    return assemble_out(res.results)


# version 05
# speedup vs baseline: 1.3434x; 1.3434x over previous
import math
from contextlib import ExitStack

import numpy as np
import ml_dtypes
import concourse.bass as bass
import concourse.mybir as mybir
from concourse.bass_utils import run_bass_kernel_spmd

F32 = mybir.dt.float32
F16 = mybir.dt.float16
BF16 = mybir.dt.bfloat16
AF = mybir.ActivationFunctionType
ALU = mybir.AluOpType
AX = mybir.AxisListType

D = 2048
NH = 16
SEQ = 8192
QB = 512
HALF_JOBS = [(3, 0), (2, 1)]
HALF_PREFIX = [8192, 6144]
NE = 10
EH = 1408
NHS = 11
SQK = 1.0 / math.sqrt(128.0)


class Sem:
    def __init__(self, h):
        self.h = h
        self.n = 0


class Prog:

    def __init__(self):
        self.ops = []

    def op(self, eng, emit, waits=(), inc=None):
        idx = len(self.ops)
        ws = []
        for w in waits:
            if w is None:
                continue
            s, v, widx = w
            assert widx < idx, f"wait on future token at op {idx}"
            ws.append((s, v))
        incs = [] if inc is None else [inc]
        for s, n in incs:
            s.n += n
        self.ops.append((eng, ws, emit, incs))
        if incs:
            s, _ = incs[0]
            return (s, s.n, idx)
        return (None, 0, idx)

    def emit_stream(self, eng_obj, name):
        hiwater = {}
        for eng, ws, emit, incs in self.ops:
            if eng != name:
                continue
            for s, v in ws:
                if hiwater.get(id(s), -1) >= v:
                    continue
                eng_obj.wait_ge(s.h, v)
                hiwater[id(s)] = v
            if emit is None:
                continue
            ins = emit(eng_obj)
            for s, n in incs:
                ins.then_inc(s.h, n)


def W(*toks):
    return [t for t in toks if t is not None]


def build_program(taps=()):
    nc = bass.Bass("TRN2", target_bir_lowering=False)

    d_x = nc.dram_tensor("xseq", [SEQ, D], F32, kind="ExternalInput").ap()
    d_wq = nc.dram_tensor("wq", [D, D], F32, kind="ExternalInput").ap()
    d_bq = nc.dram_tensor("bq", [1, D], F32, kind="ExternalInput").ap()
    d_wk = nc.dram_tensor("wk", [D, 256], F32, kind="ExternalInput").ap()
    d_bk = nc.dram_tensor("bk", [1, 256], F32, kind="ExternalInput").ap()
    d_wv = nc.dram_tensor("wv_aug", [D, 260], F32, kind="ExternalInput").ap()
    d_bv = nc.dram_tensor("bv_aug", [1, 260], F32, kind="ExternalInput").ap()
    d_wo = nc.dram_tensor("wo", [D, D], F32, kind="ExternalInput").ap()
    d_bo = nc.dram_tensor("bo", [1, D], F32, kind="ExternalInput").ap()
    d_gw = nc.dram_tensor("gwr", [128, 16, 8], F32, kind="ExternalInput").ap()
    d_wg = nc.dram_tensor("wg", [NE, D, EH], F32, kind="ExternalInput").ap()
    d_wu = nc.dram_tensor("wu", [NE, D, EH], F32, kind="ExternalInput").ap()
    d_wd = nc.dram_tensor("wd", [NE, EH, D], F32, kind="ExternalInput").ap()
    d_mask = nc.dram_tensor("maskT", [2048, QB], BF16, kind="ExternalInput").ap()
    d_id = nc.dram_tensor("ident", [128, 128], F16, kind="ExternalInput").ap()
    d_out = nc.dram_tensor("out", [2048, D], F32, kind="ExternalOutput").ap()

    tap_outs = {}
    tap_dims = {
        "xn0": ([128, 16, 1024], F32), "K0": ([128, 2, 8192], F32),
        "V0": ([128, 64, 260], F32), "Q0": ([128, 16, 1024], F32),
        "mix0": ([128, 16, 1024], F32), "acc0w": ([128, 8, 2048], F32),
        "comb0": ([128, 64], F32), "acc0": ([128, 8, 2048], F32),
        "hp00": ([128, 11, 1024], F32),
    }
    for t in taps:
        shape, dt = tap_dims[t]
        tap_outs[t] = nc.dram_tensor("tap_" + t, shape, dt,
                                     kind="ExternalOutput").ap()

    p = Prog()

    with ExitStack() as st:
        E = st.enter_context
        A = nc.alloc_sbuf_tensor_at

        BASE = 16512
        g_off = [BASE]

        def galloc(name, shape, dtype):
            per = int(np.prod(shape[1:])) * mybir.dt.size(dtype)
            t = A(name, shape, dtype, offset=g_off[0])
            g_off[0] += (per + 31) // 32 * 32
            assert g_off[0] <= BASE + 19456, (name, g_off[0])
            return t

        ones = galloc("ones", [1, 512], F16)
        identS = galloc("identS", [128, 128], F16)
        bq_row = galloc("bq_row", [1, D], F16)
        bo_row = galloc("bo_row", [1, D], F16)
        bk_row = galloc("bk_row", [1, 256], F16)
        bv_row = galloc("bv_row", [1, 260], F16)
        gw_sb = galloc("gw_sb", [128, 16, 8], F16)
        qk_col = galloc("qk_col", [128, 1], F32)
        eps_c = galloc("eps_c", [128, 1], F32)
        stats = galloc("stats", [128, 24], F32)
        mv = [galloc("mv0", [128, 2], F32), galloc("mv1", [128, 2], F32)]
        rstd0 = [galloc("rstd00", [128, 1], F32), galloc("rstd01", [128, 1], F32)]
        rstd = [galloc("rstd_0", [128, 1], F32), galloc("rstd_1", [128, 1], F32)]
        rc = [galloc("rc0", [128, 1], F32), galloc("rc1", [128, 1], F32)]
        amix_tm = [galloc("amix0", [128, 128], F16), galloc("amix1", [128, 128], F16)]
        c_negm = galloc("c_negm", [128, 1], F32)
        c_e8 = galloc("c_e8", [128, 8], F32)
        c_s8 = galloc("c_s8", [128, 1], F32)
        c_rs8 = galloc("c_rs8", [128, 1], F32)
        c_pp8 = galloc("c_pp8", [128, 8], F32)
        c_m1 = galloc("c_m1", [128, 1], F32)
        c_ismax = galloc("c_ismax", [128, 8], F32)
        c_z8 = galloc("c_z8", [128, 8], F32)
        c_m2 = galloc("c_m2", [128, 1], F32)
        c_dum = galloc("c_dum", [128, 1], F32)
        combT = galloc("combT", [128, 64], F32)
        silu_t = [galloc("silu0", [128, 512], F16), galloc("silu1", [128, 512], F16)]
        onec = galloc("onec", [128, 1], F16)
        rs_row = galloc("rs_row", [1, 512], F16)
        rs_row32 = galloc("rs_row32", [1, 512], F32)
        bcast_sb = galloc("bcast_sb", [128, 512], F16)

        R1 = BASE + 19456
        R2 = R1 + 32768
        R3 = R2 + 33280
        R4 = R3 + 32768
        R5 = R4 + 32768
        R6 = R5 + 32768
        assert R6 + 22784 <= 229344

        K_sb = A("K_sb", [128, 2, 8192], F16, offset=R1)
        V_sb = A("V_sb", [128, 64, 260], F16, offset=R2)
        acc = A("acc", [128, 8, D], F32, offset=R1)
        xn = A("xn", [128, 16, 1024], F16, offset=R3)
        x_t = [A("x_t0", [128, D], F32, offset=R4),
               A("x_t1", [128, D], F32, offset=R4 + 8192)]
        xn_tm = [A("xn_tm0", [128, D], F16, offset=R4 + 16384),
                 A("xn_tm1", [128, D], F16, offset=R4 + 20480)]
        xn_sc = [A("xn_sc0", [128, 16, 128], F16, offset=R4 + 24576),
                 A("xn_sc1", [128, 16, 128], F16, offset=R4 + 28672)]
        Q_sb = A("Q_sb", [128, 16, 1024], F16, offset=R4)
        wo_t = [A("wo_t0", [128, 16, 512], F16, offset=R4),
                A("wo_t1", [128, 16, 512], F16, offset=R4 + 16384)]
        wg_t = [A("wg_t0", [128, 16, 256], F16, offset=R4),
                A("wg_t1", [128, 16, 256], F16, offset=R4 + 8192)]
        wu_t = [A("wu_t0", [128, 16, 256], F16, offset=R4 + 16384),
                A("wu_t1", [128, 16, 256], F16, offset=R4 + 24576)]
        amix = A("amix", [128, 16, 1024], F16, offset=R5)
        hp = A("hp", [128, 11, 1024], F16, offset=R5)
        wk_sb = A("wk_sb", [128, 16, 256], F16, offset=R6)
        wv_sb = A("wv_sb", [128, 16, 260], F16, offset=R6 + 8192)
        wq_t = [A("wq_t0", [128, 16, 256], F16, offset=R6),
                A("wq_t1", [128, 16, 256], F16, offset=R6 + 8192)]
        mask_sb = A("mask_sb", [128, 16, 512], BF16, offset=R6)
        expT = [A("expT%d" % i, [128, 1024], BF16, offset=R6 + 16384 + 2048 * i)
                for i in range(2)]
        xw_t = [A("xw_t0", [128, D], F32, offset=R6),
                A("xw_t1", [128, D], F32, offset=R6 + 8192)]
        wd_t = [A("wd_t0", [128, 11, 512], F16, offset=R6),
                A("wd_t1", [128, 11, 512], F16, offset=R6 + 11264)]

        parena = E(nc.psum_tensor("parena", [128, 3072], F32))
        p32 = [parena[:, 512 * i:512 * i + 512] for i in range(6)]
        pair = [parena[:, 1024 * j:1024 * j + 1024] for j in range(2)]
        av_bank = parena[:, 2048:2560]
        sums_v = parena[0:1, 2560:3072]
        p16 = [E(nc.psum_tensor("pt%d" % i, [128, 1024], F16)) for i in range(2)]

        s_x = Sem(E(nc.semaphore("s_x")))
        s_w = Sem(E(nc.semaphore("s_w")))
        s_pe = Sem(E(nc.semaphore("s_pe")))
        s_act = Sem(E(nc.semaphore("s_act")))
        s_dve = Sem(E(nc.semaphore("s_dve")))
        s_out = Sem(E(nc.semaphore("s_out")))

        p.op("dve", lambda e: e.memset(ones[:], 1.0))
        p.op("dve", lambda e: e.memset(onec[:], 1.0))
        p.op("dve", lambda e: e.memset(qk_col[:], SQK))
        tok_init = p.op("dve", lambda e: e.memset(eps_c[:], 1e-5), inc=(s_dve, 1))

        def gdma(dst, src, waits=()):
            return p.op("pool", lambda e, d=dst, s=src: e.dma_start(d, s),
                        waits=waits, inc=(s_w, 16))

        gdma(bq_row[:], d_bq[:, :])
        gdma(bo_row[:], d_bo[:, :])
        gdma(bk_row[:], d_bk[:, :])
        gdma(bv_row[:], d_bv[:, :])
        gdma(gw_sb[:], d_gw[:, :, :])
        tok_globals = gdma(identS[:], d_id[:, :])

        free = {}
        tap_done = set()

        def tap(name, sbuf_ap, wait_tok):
            if name in tap_outs and name not in tap_done:
                tap_done.add(name)
                p.op("pool", lambda e, n=name, a=sbuf_ap: e.dma_start(
                    tap_outs[n][...], a), waits=[wait_tok], inc=(s_out, 16))

        tpsum_free = {0: [], 1: []}

        for half in range(2):
            mlist = HALF_JOBS[half]
            prefix = HALF_PREFIX[half]
            ntiles = prefix // 128
            own_blocks = [4 * mlist[0], 4 * mlist[1]]
            own_tiles = {}
            for j, b in enumerate(own_blocks):
                for tb in range(4):
                    own_tiles[4 * b + tb] = 512 * j + 128 * tb

            if half == 1:
                bar = (s_out, 16 * 1 + 16 * len(tap_done), len(p.ops) - 1)
                for eng in ("sp", "pool", "pe", "act", "dve"):
                    p.op(eng, None, waits=[bar])

            tok_wk = gdma(wk_sb[:], d_wk.rearrange("(c p) d -> p c d", p=128))
            tok_wv = gdma(wv_sb[:], d_wv.rearrange("(c p) d -> p c d", p=128))

            kpsum_free = {0: [], 1: []}
            vpsum_free = {0: [], 1: []}
            kv_evacs = []
            last_xn_evac = None
            own_tiles2 = own_tiles

            def kv_stageA(i):
                sl = i % 2
                tok_x = p.op("sp", lambda e, sl=sl, i=i: e.dma_start(
                    x_t[sl][:], d_x[128 * i:128 * i + 128, :]),
                    waits=W(free.get("x_t%d" % sl)), inc=(s_x, 16))
                tok_st = None
                for k in range(4):
                    tok_st = p.op("dve", lambda e, sl=sl, k=k: e.bn_stats(
                        stats[:, 6 * k:6 * k + 6],
                        x_t[sl][:, 512 * k:512 * k + 512]),
                        waits=[tok_x] if k == 0 else (),
                        inc=(s_dve, 1) if k == 3 else None)
                tok_ag = p.op("dve", lambda e, sl=sl: e.bn_aggr(mv[sl][:], stats[:]),
                              waits=W(tok_st, free.get("mv%d" % sl)),
                              inc=(s_dve, 1))
                tok_sq = p.op("act", lambda e, sl=sl: e.activation(
                    rstd0[sl][:], mv[sl][:, 1:2], AF.Sqrt, bias=eps_c[:]),
                    waits=W(tok_ag, tok_init, free.get("rstd0%d" % sl)),
                    inc=(s_act, 1))
                free["mv%d" % sl] = tok_sq
                tok_rc = p.op("dve", lambda e, sl=sl: e.reciprocal(
                    rstd[sl][:], rstd0[sl][:]), waits=[tok_sq], inc=(s_dve, 1))
                free["rstd0%d" % sl] = tok_rc
                tok_ts = p.op("dve", lambda e, sl=sl: e.tensor_scalar(
                    xn_tm[sl][:], x_t[sl][:], mv[sl][:, 0:1], rstd[sl][:],
                    ALU.subtract, ALU.mult),
                    waits=W(tok_rc, free.get("xn_tm%d" % sl)), inc=(s_dve, 1))
                free["x_t%d" % sl] = tok_ts
                return tok_ts

            def kv_stageB(i, tok_ts):
                nonlocal last_xn_evac, kv_evacs
                sl = i % 2
                if i in own_tiles2:
                    off = own_tiles2[i]
                    dst_ap = lambda c0, off=off: xn[:, c0:c0 + 8, off:off + 128]
                    src_ = lambda fc, off=off: xn[:, fc, off:off + 128]
                else:
                    dst_ap = lambda c0, sl=sl: xn_sc[sl][:, c0:c0 + 8, :]
                    src_ = lambda fc, sl=sl: xn_sc[sl][:, fc, :]
                tr_toks = []
                for b16 in range(2):
                    wt = W(tok_ts, tok_globals) + tpsum_free[b16]
                    tk = None
                    for k in range(8):
                        fc = 8 * b16 + k
                        tk = p.op("pe", lambda e, sl=sl, b16=b16, k=k, fc=fc:
                                  e.transpose(p16[b16][:, 128 * k:128 * k + 128],
                                              xn_tm[sl][:, 128 * fc:128 * fc + 128],
                                              identS[:]),
                                  waits=wt if k == 0 else (),
                                  inc=(s_pe, 1) if k == 7 else None)
                    tr_toks.append(tk)
                free["xn_tm%d" % sl] = tr_toks[1]
                ev_last = None
                for b16 in range(2):
                    wt = [tr_toks[b16]]
                    if i in own_tiles2:
                        wt = wt + W(free.get("xn_own"))
                    ev_last = p.op("dve", lambda e, b16=b16, dst_ap=dst_ap:
                                   e.tensor_copy(
                                       dst_ap(8 * b16),
                                       p16[b16].rearrange("p (c t) -> p c t", c=8)),
                                   waits=wt, inc=(s_dve, 1))
                    tpsum_free[b16] = [ev_last]
                xn_ready = ev_last
                last_xn_evac = ev_last

                par = i % 2
                for h in range(2):
                    wt = W(xn_ready, tok_wk) + kpsum_free[par]
                    tk = None
                    p.op("pe", lambda e, h=h, par=par: e.matmul(
                        p32[[0, 2][h] + par][:, 0:128],
                        bk_row[0:1, 128 * h:128 * h + 128], ones[0:1, 0:128],
                        start=True, stop=False), waits=wt)
                    for fc in range(16):
                        tk = p.op("pe", lambda e, h=h, par=par, fc=fc, src_=src_:
                                  e.matmul(p32[[0, 2][h] + par][:, 0:128],
                                           wk_sb[:, fc, 128 * h:128 * h + 128],
                                           src_(fc), start=False, stop=(fc == 15)),
                                  inc=(s_pe, 1) if fc == 15 else None)
                    tok_kev = p.op("act", lambda e, h=h, par=par, i=i: e.activation(
                        K_sb[:, h, 128 * i:128 * i + 128],
                        p32[[0, 2][h] + par][:, 0:128], AF.Copy),
                        waits=[tk], inc=(s_act, 1))
                    if h == 0:
                        kpsum_free[par] = [tok_kev]
                    else:
                        kpsum_free[par] = kpsum_free[par] + [tok_kev]
                    kv_evacs.append(tok_kev)
                wt = W(xn_ready, tok_wv) + vpsum_free[par]
                p.op("pe", lambda e, par=par: e.matmul(
                    p32[4 + par][:, 0:260], ones[0:1, 0:128], bv_row[0:1, :],
                    start=True, stop=False), waits=wt)
                tk = None
                for fc in range(16):
                    tk = p.op("pe", lambda e, par=par, fc=fc, src_=src_: e.matmul(
                        p32[4 + par][:, 0:260], src_(fc), wv_sb[:, fc, :],
                        start=False, stop=(fc == 15)),
                        inc=(s_pe, 1) if fc == 15 else None)
                tok_vev = p.op("dve", lambda e, par=par, i=i: e.tensor_copy(
                    V_sb[:, i, :], p32[4 + par][:, 0:260]),
                    waits=[tk], inc=(s_dve, 1))
                vpsum_free[par] = [tok_vev]
                kv_evacs.append(tok_vev)
                return tok_vev

            ts_tok = {0: kv_stageA(0)}
            for i in range(ntiles):
                if i + 1 < ntiles:
                    ts_tok[i + 1] = kv_stageA(i + 1)
                tok_vev = kv_stageB(i, ts_tok.pop(i))
            free["xn_own"] = None
            kv_last_dve = tok_vev
            kv_last_act = kv_evacs[-2]
            if half == 0:
                tap("xn0", xn[:, :, :], last_xn_evac)
                tap("K0", K_sb[:, :, :], kv_last_act)
                tap("V0", V_sb[:, :, :], kv_last_dve)

            qbank_free = {k: kpsum_free[0] + kpsum_free[1] for k in range(4)}
            qbank_free[4] = vpsum_free[0]
            qbank_free[5] = vpsum_free[1]
            q_done = None
            bk_ctr = 0
            for qc in range(8):
                sl = qc % 2
                wqw = W(free.get("wq%d" % sl))
                if qc < 2:
                    wqw = wqw + [kv_last_dve]
                tok_wq = gdma(wq_t[sl][:],
                              d_wq[:, 256 * qc:256 * qc + 256]
                              .rearrange("(c p) q -> p c q", p=128),
                              waits=wqw)
                for hl in range(2):
                    hd = 2 * qc + hl
                    for tg in range(2):
                        bk = bk_ctr % 4
                        bk_ctr += 1
                        wt = W(tok_wq, kv_last_dve, kv_last_act) + qbank_free[bk]
                        p.op("pe", lambda e, hd=hd, bk=bk: e.matmul(
                            p32[bk][:, 0:512],
                            bq_row[0:1, 128 * hd:128 * hd + 128],
                            ones[0:1, 0:512], start=True, stop=False), waits=wt)
                        tk = None
                        for fc in range(16):
                            tk = p.op("pe", lambda e, bk=bk, fc=fc, sl=sl, hl=hl,
                                      tg=tg: e.matmul(
                                p32[bk][:, 0:512],
                                wq_t[sl][:, fc, 128 * hl:128 * hl + 128],
                                xn[:, fc, 512 * tg:512 * tg + 512],
                                start=False, stop=(fc == 15)),
                                inc=(s_pe, 1) if fc == 15 else None)
                        free["wq%d" % sl] = tk
                        tok_qe = p.op("act", lambda e, hd=hd, bk=bk, tg=tg:
                                      e.activation(
                                          Q_sb[:, hd, 512 * tg:512 * tg + 512],
                                          p32[bk][:, 0:512], AF.Copy,
                                          scale=qk_col[:]),
                                      waits=[tk, tok_init], inc=(s_act, 1))
                        qbank_free[bk] = [tok_qe]
            q_last = tok_qe
            if half == 0:
                tap("Q0", Q_sb[:, :, :], q_last)

            tok_mask = gdma(mask_sb[:], d_mask.rearrange("(c p) q -> p c q", p=128),
                            waits=W(free.get("wq0"), free.get("wq1")))
            pair_free = {0: qbank_free[0] + qbank_free[1],
                         1: qbank_free[2] + qbank_free[3]}
            av_free = vpsum_free[0]
            sums_free = vpsum_free[1]
            expT_free = {k: [] for k in range(2)}
            tok_me = None
            for slot in range(2):
                m = mlist[slot]
                kcs = (2048 * (m + 1)) // 128
                qb0 = 512 * slot
                for hd in range(NH):
                    kvh = hd // 8
                    tok_av_last = None
                    tok_sum_last = None
                    pend = None

                    def emit_avsums(tok_rdy, kp, kvh=kvh, kcs=kcs):
                        nonlocal tok_av_last, tok_sum_last
                        eb = kp % 2
                        tk = None
                        for half_k in range(2):
                            kc = 2 * kp + half_k
                            wt2 = [tok_rdy]
                            if kc == 0:
                                wt2 = wt2 + av_free
                            tk = p.op("pe", lambda e, eb=eb, kc=kc, kvh=kvh,
                                      hk=half_k, kcs=kcs: e.matmul(
                                av_bank[:, 0:512],
                                V_sb[:, kc, 130 * kvh:130 * kvh + 128],
                                expT[eb][:, 512 * hk:512 * hk + 512],
                                start=(kc == 0), stop=(kc == kcs - 1)),
                                waits=wt2, inc=(s_pe, 1) if kc == kcs - 1 else None)
                            wt3 = []
                            if kc == 0:
                                wt3 = sums_free
                            tok_sum_last = p.op("pe", lambda e, eb=eb, kc=kc,
                                                hk=half_k, kcs=kcs: e.matmul(
                                sums_v, onec[:, 0:1],
                                expT[eb][:, 512 * hk:512 * hk + 512],
                                start=(kc == 0), stop=(kc == kcs - 1)),
                                waits=wt3,
                                inc=(s_pe, 1) if half_k == 1 else None)
                        if kc == kcs - 1:
                            tok_av_last = tk
                        expT_free[eb] = [tok_sum_last]

                    for kp in range(kcs // 2):
                        pb = kp % 2
                        eb = kp % 2
                        wt = W(q_last, kv_last_dve) + pair_free[pb]
                        for half_k in range(2):
                            kc = 2 * kp + half_k
                            tok_qk = p.op("pe", lambda e, kvh=kvh, hd=hd, kc=kc,
                                          pb=pb, hk=half_k, qb0=qb0: e.matmul(
                                pair[pb][:, 512 * hk:512 * hk + 512],
                                K_sb[:, kvh, 128 * kc:128 * kc + 128],
                                Q_sb[:, hd, qb0:qb0 + 512], start=True, stop=True),
                                waits=wt if half_k == 0 else (),
                                inc=(s_pe, 1) if half_k == 1 else None)
                        tok_ex = p.op("act", lambda e, pb=pb, eb=eb: e.activation(
                            expT[eb][:], pair[pb][:, 0:1024], AF.Exp),
                            waits=[tok_qk] + expT_free[eb], inc=(s_act, 1))
                        pair_free[pb] = [tok_ex]
                        tok_rdy = tok_ex
                        if 2 * kp >= kcs - 16:
                            mi = 2 * kp - (kcs - 16)
                            tok_rdy = p.op("dve", lambda e, eb=eb, mi=mi:
                                           e.tensor_tensor(
                                               expT[eb][:], expT[eb][:],
                                               mask_sb[:, mi:mi + 2, :]
                                               .rearrange("p c q -> p (c q)"),
                                               ALU.mult),
                                           waits=[tok_ex, tok_mask],
                                           inc=(s_dve, 1))
                        if pend is not None:
                            emit_avsums(*pend)
                        pend = (tok_rdy, kp)
                    emit_avsums(*pend)
                    tok_rr = p.op("dve", lambda e: e.reciprocal(
                        rs_row32[:], sums_v), waits=W(tok_sum_last,
                                                      free.get("rs_row32")),
                        inc=(s_dve, 1))
                    sums_free = [tok_rr]
                    tok_rr = p.op("act", lambda e: e.activation(
                        rs_row[:], rs_row32[:], AF.Copy),
                        waits=W(tok_rr, free.get("rs_row")), inc=(s_act, 1))
                    free["rs_row32"] = tok_rr
                    bcb = (kcs // 2) % 2
                    tok_bc = p.op("pe", lambda e, bcb=bcb: e.matmul(
                        pair[bcb][:, 0:512], ones[0:1, 0:128], rs_row[:],
                        start=True, stop=True),
                        waits=[tok_rr] + pair_free[bcb], inc=(s_pe, 1))
                    tok_bcc = p.op("act", lambda e, bcb=bcb: e.activation(
                        bcast_sb[:], pair[bcb][:, 0:512], AF.Copy),
                        waits=W(tok_bc, free.get("bcast_sb")), inc=(s_act, 1))
                    free["rs_row"] = tok_bcc
                    pair_free[bcb] = [tok_bcc]
                    tok_me = p.op("dve", lambda e, hd=hd, qb0=qb0:
                                  e.tensor_tensor(
                                      amix[:, hd, qb0:qb0 + 512],
                                      av_bank[:, 0:512], bcast_sb[:],
                                      ALU.mult),
                                  waits=W(tok_av_last, tok_bcc), inc=(s_dve, 1))
                    free["bcast_sb"] = tok_me
                    av_free = [tok_me]
            mix_last = tok_me
            if half == 0:
                tap("mix0", amix[:, :, :], mix_last)

            wo_bank_free = {k: [] for k in range(6)}
            wo_pe_last = None
            tok_ae = None
            for dmc in range(4):
                sl = dmc % 2
                tok_wo = gdma(wo_t[sl][:],
                              d_wo[:, 512 * dmc:512 * dmc + 512]
                              .rearrange("(c p) q -> p c q", p=128),
                              waits=W(free.get("wo%d" % sl), mix_last))
                for t8 in range(8):
                    bk = (dmc * 8 + t8) % 6
                    wt = W(tok_wo, mix_last) + wo_bank_free[bk]
                    p.op("pe", lambda e, dmc=dmc, bk=bk: e.matmul(
                        p32[bk][:, 0:512], ones[0:1, 0:128],
                        bo_row[0:1, 512 * dmc:512 * dmc + 512],
                        start=True, stop=False), waits=wt)
                    tk = None
                    for fc in range(16):
                        tk = p.op("pe", lambda e, bk=bk, fc=fc, sl=sl, t8=t8:
                                  e.matmul(p32[bk][:, 0:512],
                                           amix[:, fc, 128 * t8:128 * t8 + 128],
                                           wo_t[sl][:, fc, :],
                                           start=False, stop=(fc == 15)),
                                  inc=(s_pe, 1) if fc == 15 else None)
                    free["wo%d" % sl] = tk
                    wo_pe_last = tk
                    tok_ae = p.op("dve", lambda e, bk=bk, t8=t8, dmc=dmc:
                                  e.tensor_copy(
                                      acc[:, t8, 512 * dmc:512 * dmc + 512],
                                      p32[bk][:, 0:512]),
                                  waits=[tk], inc=(s_dve, 1))
                    wo_bank_free[bk] = [tok_ae]
            acc_w_last = tok_ae
            tok_ax = None
            for t8 in range(8):
                sl = t8 % 2
                j, tb = t8 // 4, t8 % 4
                row0 = 512 * own_blocks[j] + 128 * tb
                tok_xw = p.op("sp", lambda e, sl=sl, row0=row0: e.dma_start(
                    xw_t[sl][:], d_x[row0:row0 + 128, :]),
                    waits=W(free.get("xw%d" % sl), mix_last), inc=(s_x, 16))
                tok_ax = p.op("dve", lambda e, sl=sl, t8=t8: e.tensor_tensor(
                    acc[:, t8, :], acc[:, t8, :], xw_t[sl][:], ALU.add),
                    waits=[tok_xw, acc_w_last], inc=(s_dve, 1))
                free["xw%d" % sl] = tok_ax
            accx_last = tok_ax
            if half == 0:
                tap("acc0w", acc[:, :, :], accx_last)

            comb_prev = None
            tok_cb = None
            for t8 in range(8):
                bk = t8 % 2
                wt = W(tok_globals, comb_prev) + wo_bank_free[bk]
                tk = None
                for fc in range(16):
                    tk = p.op("pe", lambda e, bk=bk, fc=fc, t8=t8: e.matmul(
                        p32[bk][:, 0:8], xn[:, fc, 128 * t8:128 * t8 + 128],
                        gw_sb[:, fc, :], start=(fc == 0), stop=(fc == 15)),
                        waits=wt if fc == 0 else (),
                        inc=(s_pe, 1) if fc == 15 else None)
                tok_nm = p.op("dve", lambda e, bk=bk: e.reduce_max(
                    c_negm[:], p32[bk][:, 0:8], AX.X, negate=True),
                    waits=W(tk, comb_prev), inc=(s_dve, 1))
                tok_e8 = p.op("act", lambda e, bk=bk: e.activation(
                    c_e8[:], p32[bk][:, 0:8], AF.Exp, bias=c_negm[:],
                    accum_out=c_s8[:]), waits=[tok_nm], inc=(s_act, 1))
                tok_e8b = p.op("act", lambda e: e.activation(
                    c_dum[:], c_negm[:], AF.Square), waits=[tok_e8],
                    inc=(s_act, 1))
                wo_bank_free[bk] = [tok_e8b]
                tok_rs = p.op("dve", lambda e: e.reciprocal(c_rs8[:], c_s8[:]),
                              waits=[tok_e8b], inc=(s_dve, 1))
                tok_pp = p.op("dve", lambda e: e.tensor_scalar(
                    c_pp8[:], c_e8[:], c_rs8[:], None, ALU.mult),
                    waits=[tok_rs], inc=(s_dve, 1))
                tok_m1 = p.op("dve", lambda e: e.reduce_max(
                    c_m1[:], c_pp8[:], AX.X), waits=[tok_pp], inc=(s_dve, 1))
                tok_im = p.op("dve", lambda e: e.tensor_scalar(
                    c_ismax[:], c_pp8[:], c_m1[:], None, ALU.is_ge),
                    waits=[tok_m1], inc=(s_dve, 1))
                tok_z8 = p.op("dve", lambda e: e.scalar_tensor_tensor(
                    c_z8[:], c_ismax[:], -1e30, c_pp8[:], ALU.mult, ALU.add),
                    waits=[tok_im], inc=(s_dve, 1))
                tok_m2 = p.op("dve", lambda e: e.reduce_max(
                    c_m2[:], c_z8[:], AX.X), waits=[tok_z8], inc=(s_dve, 1))
                tok_cb = p.op("dve", lambda e, t8=t8: e.scalar_tensor_tensor(
                    combT[:, 8 * t8:8 * t8 + 8], c_pp8[:], c_m2[:], c_pp8[:],
                    ALU.is_ge, ALU.mult), waits=[tok_m2], inc=(s_dve, 1))
                comb_prev = tok_cb
            comb_last = tok_cb
            if half == 0:
                tap("comb0", combT[:, :], comb_last)

            gu_bank_free = {k: [comb_last] for k in range(4)}
            dn_bank_free = {4: wo_bank_free.get(4, []) + [comb_last],
                            5: wo_bank_free.get(5, []) + [comb_last]}
            hp_free = [wo_pe_last]
            acc_chain = [accx_last, comb_last]
            tok_de = None
            for e_i in range(NE):
                hcl = [(256 * c, 256 if c < 5 else 128) for c in range(6)]
                tok_hp = None
                first_gu = (e_i == 0)
                for ci, (hoff, cw) in enumerate(hcl):
                    sl = ci % 2
                    wgw = W(free.get("wg%d" % sl))
                    wuw = W(free.get("wu%d" % sl))
                    if first_gu and ci < 2:
                        wgw += W(free.get("wo0"), free.get("wo1"))
                        wuw += W(free.get("wo0"), free.get("wo1"))
                    tok_wg = gdma(wg_t[sl][:, :, 0:cw],
                                  d_wg[e_i, :, hoff:hoff + cw]
                                  .rearrange("(c p) h -> p c h", p=128), waits=wgw)
                    tok_wu = gdma(wu_t[sl][:, :, 0:cw],
                                  d_wu[e_i, :, hoff:hoff + cw]
                                  .rearrange("(c p) h -> p c h", p=128), waits=wuw)
                    for hs in range(cw // 128):
                        hglob = hoff // 128 + hs
                        for tg in range(2):
                            gb = 2 * ((hglob + tg) % 2)
                            spr = (hglob + tg) % 2
                            wt = W(tok_wg, comb_last) + gu_bank_free[gb]
                            tkg = None
                            for fc in range(16):
                                tkg = p.op("pe", lambda e, gb=gb, fc=fc, sl=sl,
                                           hs=hs, tg=tg: e.matmul(
                                    p32[gb][:, 0:512],
                                    wg_t[sl][:, fc, 128 * hs:128 * hs + 128],
                                    xn[:, fc, 512 * tg:512 * tg + 512],
                                    start=(fc == 0), stop=(fc == 15)),
                                    waits=wt if fc == 0 else (),
                                    inc=(s_pe, 1) if fc == 15 else None)
                            wt = W(tok_wu) + gu_bank_free[gb + 1]
                            tku = None
                            for fc in range(16):
                                tku = p.op("pe", lambda e, gb=gb, fc=fc, sl=sl,
                                           hs=hs, tg=tg: e.matmul(
                                    p32[gb + 1][:, 0:512],
                                    wu_t[sl][:, fc, 128 * hs:128 * hs + 128],
                                    xn[:, fc, 512 * tg:512 * tg + 512],
                                    start=(fc == 0), stop=(fc == 15)),
                                    waits=wt if fc == 0 else (),
                                    inc=(s_pe, 1) if fc == 15 else None)
                            free["wg%d" % sl] = tkg
                            free["wu%d" % sl] = tku
                            tok_si = p.op("act", lambda e, gb=gb, spr=spr:
                                          e.activation(silu_t[spr][:],
                                                       p32[gb][:, 0:512], AF.Silu),
                                          waits=W(tkg, free.get("silu%d" % spr)),
                                          inc=(s_act, 1))
                            gu_bank_free[gb] = [tok_si]
                            tok_hp = p.op("dve", lambda e, hglob=hglob, tg=tg,
                                          gb=gb, spr=spr: e.tensor_tensor(
                                hp[:, hglob, 512 * tg:512 * tg + 512],
                                silu_t[spr][:], p32[gb + 1][:, 0:512], ALU.mult),
                                waits=W(tku, tok_si) + hp_free, inc=(s_dve, 1))
                            free["silu%d" % spr] = tok_hp
                            gu_bank_free[gb + 1] = [tok_hp]
                hp_ready = tok_hp
                if half == 0 and e_i == 0:
                    tap("hp00", hp[:, :, :], hp_ready)
                hp_pe_last = None
                for dmc in range(4):
                    sl = dmc % 2
                    wdw = W(free.get("wd%d" % sl))
                    if e_i == 0 and dmc < 2:
                        wdw += [mix_last, accx_last]
                    tok_wd = gdma(wd_t[sl][:],
                                  d_wd[e_i, :, 512 * dmc:512 * dmc + 512]
                                  .rearrange("(c p) d -> p c d", p=128), waits=wdw)
                    for t8 in range(8):
                        bk = 4 + (t8 % 2)
                        wt = W(tok_wd, hp_ready) + dn_bank_free[bk]
                        tk = None
                        for hs in range(NHS):
                            tk = p.op("pe", lambda e, bk=bk, hs=hs, t8=t8, sl=sl:
                                      e.matmul(p32[bk][:, 0:512],
                                               hp[:, hs, 128 * t8:128 * t8 + 128],
                                               wd_t[sl][:, hs, :],
                                               start=(hs == 0),
                                               stop=(hs == NHS - 1)),
                                      waits=wt if hs == 0 else (),
                                      inc=(s_pe, 1) if hs == NHS - 1 else None)
                        hp_pe_last = tk
                        free["wd%d" % sl] = tk
                        wt = [tk] + acc_chain
                        acc_chain = []
                        if e_i < 8:
                            tok_de = p.op("dve", lambda e, bk=bk, t8=t8, dmc=dmc,
                                          e_i=e_i: e.scalar_tensor_tensor(
                                acc[:, t8, 512 * dmc:512 * dmc + 512],
                                p32[bk][:, 0:512],
                                combT[:, 8 * t8 + e_i:8 * t8 + e_i + 1],
                                acc[:, t8, 512 * dmc:512 * dmc + 512],
                                ALU.mult, ALU.add), waits=wt, inc=(s_dve, 1))
                        else:
                            tok_de = p.op("dve", lambda e, bk=bk, t8=t8, dmc=dmc:
                                          e.tensor_tensor(
                                acc[:, t8, 512 * dmc:512 * dmc + 512],
                                acc[:, t8, 512 * dmc:512 * dmc + 512],
                                p32[bk][:, 0:512], ALU.add), waits=wt,
                                inc=(s_dve, 1))
                        dn_bank_free[bk] = [tok_de]
                hp_free = [hp_pe_last]
                acc_chain = [tok_de]
            ffn_last = tok_de
            if half == 0:
                tap("acc0", acc[:, :, :], ffn_last)

            out_view = d_out[1024 * half:1024 * half + 1024, :] \
                .rearrange("(t p) d -> p t d", p=128)
            tok_out = p.op("sp", lambda e, ov=out_view: e.dma_start(ov, acc[:]),
                           waits=[ffn_last], inc=(s_out, 16))
            for nm in ("x_t0", "x_t1", "xw0", "xw1", "wo0", "wo1", "wg0", "wg1",
                       "wu0", "wu1", "wd0", "wd1", "wq0", "wq1", "xn_tm0",
                       "xn_tm1", "mv0", "mv1", "rstd00", "rstd01", "rc0", "rc1",
                       "amix0", "amix1", "silu0", "silu1", "xn_own"):
                free[nm] = None
            tpsum_free = {0: [], 1: []}

        p.op("sp", None, waits=[(s_out, s_out.n, len(p.ops) - 1)])

        with nc.Block() as blk:
            @blk.sync
            def _(e):
                p.emit_stream(e, "sp")

            @blk.gpsimd
            def _(e):
                p.emit_stream(e, "pool")

            @blk.tensor
            def _(e):
                p.emit_stream(e, "pe")

            @blk.scalar
            def _(e):
                p.emit_stream(e, "act")

            @blk.vector
            def _(e):
                p.emit_stream(e, "dve")

    return nc


def make_in_maps(inputs):
    f32 = lambda a: np.ascontiguousarray(np.asarray(a, dtype=np.float32))
    x = f32(inputs["L0_attn_x"])
    wq = f32(inputs["L0_attn_Wq"])
    bq = f32(inputs["L0_attn_bq"]).reshape(1, D)
    wk = f32(inputs["L0_attn_Wk"])
    bk = f32(inputs["L0_attn_bk"]).reshape(1, 256)
    wv = f32(inputs["L0_attn_Wv"])
    bv = f32(inputs["L0_attn_bv"]).reshape(256)
    wo = f32(inputs["L0_attn_Wo"])
    bo = f32(inputs["L0_attn_bo"]).reshape(1, D)
    gate_w = f32(inputs["L0_ffn_gate_w"])
    w_gate = f32(inputs["L0_ffn_w_gate"])
    w_up = f32(inputs["L0_ffn_w_up"])
    w_down = f32(inputs["L0_ffn_w_down"])
    sw_gate = f32(inputs["L0_ffn_sw_gate"])
    sw_up = f32(inputs["L0_ffn_sw_up"])
    sw_down = f32(inputs["L0_ffn_sw_down"])

    wv_aug = np.zeros((D, 260), np.float32)
    wv_aug[:, 0:128] = wv[:, 0:128]
    wv_aug[:, 130:258] = wv[:, 128:256]
    bv_aug = np.zeros((1, 260), np.float32)
    bv_aug[0, 0:128] = bv[0:128]
    bv_aug[0, 128] = 1.0
    bv_aug[0, 130:258] = bv[128:256]
    bv_aug[0, 258] = 1.0
    gwr = np.ascontiguousarray(gate_w.reshape(16, 128, 8).transpose(1, 0, 2))
    wg = np.ascontiguousarray(np.concatenate(
        [w_gate, sw_gate[:, 0:EH][None], sw_gate[:, EH:2 * EH][None]], 0))
    wu = np.ascontiguousarray(np.concatenate(
        [w_up, sw_up[:, 0:EH][None], sw_up[:, EH:2 * EH][None]], 0))
    wd = np.ascontiguousarray(np.concatenate(
        [w_down, sw_down[0:EH, :][None], sw_down[EH:2 * EH, :][None]], 0))
    ident = np.eye(128, dtype=np.float16)

    shared = dict(wq=wq, bq=bq, wk=wk, bk=bk, wv_aug=wv_aug, bv_aug=bv_aug,
                  wo=wo, bo=bo, gwr=gwr, wg=wg, wu=wu, wd=wd, ident=ident)
    in_maps = []
    kq = np.arange(512)
    for c in range(8):
        s, r = c // 4, c % 4
        perm = np.arange(16)
        if r:
            for m in range(4):
                perm[4 * m], perm[4 * m + r] = perm[4 * m + r], perm[4 * m]
        xp = np.ascontiguousarray(
            x[s].reshape(16, 512, D)[perm].reshape(SEQ, D))
        mask = np.zeros((2048, 512), np.float32)
        mask[0:512] = (kq[:, None] <= kq[None, :])
        for j in range(1, 4):
            mask[512 * j:512 * j + 512] = 1.0 if j <= r else 0.0
        m16 = mask.astype(ml_dtypes.bfloat16)
        im = dict(shared)
        im["xseq"] = xp
        im["maskT"] = m16
        in_maps.append(im)
    return in_maps


def assemble_out(results):
    out = np.empty((2, SEQ, D), np.float32)
    for c in range(8):
        o = np.asarray(results[c]["out"], np.float32)
        s, r = c // 4, c % 4
        for row0, m in ((0, 3), (512, 0), (1024, 2), (1536, 1)):
            b = 4 * m + r
            out[s, 512 * b:512 * b + 512] = o[row0:row0 + 512]
    return out


def kernel(**inputs):
    in_maps = make_in_maps(inputs)
    nc = build_program()
    res = run_bass_kernel_spmd(nc, in_maps, core_ids=list(range(8)))
    return assemble_out(res.results)
